# version 05 (final)
# baseline (speedup 1.0000x reference)
import numpy as np
import concourse.bass as bass
import concourse.mybir as mybir
from concourse.bass_utils import run_bass_kernel_spmd

F32 = mybir.dt.float32
BF16 = mybir.dt.bfloat16
I16 = mybir.dt.int16
ALU = mybir.AluOpType
ACT = mybir.ActivationFunctionType
AX = mybir.AxisListType

C23 = 12582912.0
C52 = 6755399441055744.0
LN_EPS = 1e-5
SUBSTOP = None
DIAG_SKIP = set()

T = 2048
D = 2048
HID = 8192
NT = T // 128
ND = D // 128
NHC = HID // 128

ENGS = ("sync", "tensor", "vector", "scalar", "gpsimd")


ESEM = {"tensor": "TE", "vector": "VE", "scalar": "SE", "gpsimd": "GE"}


class Plan:

    def __init__(self, nc, validate=False):
        self.nc = nc
        self.validate = validate
        self.ops = {e: [] for e in ENGS}
        self.sems = {}
        self.cnt = {}
        self._guards = []
        self.free = {}

    def sem(self, name):
        if name not in self.sems:
            g = self.nc.semaphore(name)
            self.sems[name] = g.__enter__()
            self._guards.append(g)
            self.cnt[name] = 0
        return self.sems[name]

    def emit(self, eng, fn, wait=(), inc=None, inc_n=1, is_dma=False):
        sems = self.sems
        esem = None if is_dma else ESEM.get(eng)
        wait = list(wait)
        if esem is not None:
            wait.append((esem, self.c(esem)))
            self.sem(esem)
            self.cnt[esem] += 1
        if inc is not None and inc != esem:
            self.sem(inc)
            self.cnt[inc] += inc_n
        ww = {}
        for s, v in wait:
            if v > 0:
                ww[s] = max(ww.get(s, 0), v)
        ww = tuple(ww.items())

        def run(e, fn=fn, ww=ww, inc=inc, inc_n=inc_n, esem=esem):
            for s, v in ww:
                e.wait_ge(sems[s], v)
            ins = fn(e)
            if esem is not None:
                ins.then_inc(sems[esem], 1)
            if inc is not None and inc != esem:
                ins.then_inc(sems[inc], inc_n)
        self.ops[eng].append(run)
        return self.cnt[inc] if inc is not None else None

    def dma(self, queue, out, in_, chan, wait=(), transpose=False,
            slow=False):
        sem = "Q_" + chan
        self.sem(sem)

        def fn(e):
            return e.dma_start(out=out, in_=in_, transpose=transpose,
                               allow_slow_non_contiguous=slow)
        return self.emit(queue, fn, wait=wait, inc=sem, inc_n=16, is_dma=True)

    def qc(self, chan):
        return ("Q_" + chan, self.c("Q_" + chan))

    def c(self, name):
        return self.cnt.get(name, 0)

    def acq(self, *res):
        out = []
        for r in res:
            out.extend(self.free.get(r, ()))
        return out

    def rel(self, res, *pairs):
        self.free[res] = tuple(pairs)

    def relmax(self, res, *pairs):
        cur = dict(self.free.get(res, ()))
        for s, v in pairs:
            cur[s] = max(cur.get(s, 0), v)
        self.free[res] = tuple(cur.items())


def build_program(validate=False, phases=3):
    nc = bass.Bass("TRN2", target_bir_lowering=False,
                   detect_race_conditions=validate)

    def din(name, shape):
        return nc.dram_tensor(name, shape, F32, kind="ExternalInput").ap()
    x_in = din("x", [T, D])
    wq_d = din("wq", [D, D]); wk_d = din("wk", [D, D])
    wv_d = din("wv", [D, D]); wo_d = din("wo", [D, D])
    bq_d = din("bq", [D]); bk_d = din("bk", [D])
    bv_d = din("bv", [D]); bo_d = din("bo", [D])
    w1_d = din("w1", [HID, D]); w2_d = din("w2", [D, HID])
    out_d = nc.dram_tensor("out", [T, D], F32, kind="ExternalOutput").ap()

    def dscr(name, shape, dt):
        return nc.dram_tensor(name, shape, dt, kind="Internal").ap()
    xb0 = dscr("xb0", [T, D], F32)
    xb1 = dscr("xb1", [T, D], F32)
    t1n = dscr("t1n", [HID, D], BF16)
    t2n = dscr("t2n", [D, HID], BF16)
    t1t = dscr("t1t", [D, HID], BF16)
    t2t = dscr("t2t", [HID, D], BF16)
    hsp = dscr("hsp", [HID, T], F32)
    ascr = dscr("ascr", [T, D], BF16)
    yscr = dscr("yscr", [D, T], BF16)
    cscr = dscr("cscr", [D, T], BF16)
    pscr = dscr("pscr", [D, T], F32)

    sbA = nc.alloc_sbuf_tensor
    AB1 = sbA("AB1", [128, 32 * 1024], BF16)
    AB2 = sbA("AB2", [128, 12352], BF16)
    AB3 = sbA("AB3", [128, 8 * 1024], BF16)
    AF1 = sbA("AF1", [128, 6 * 2048], F32)
    yst = sbA("yst", [128, 2 * T], BF16)
    wbf = sbA("wbf", [128, 2 * D], BF16)
    hst = wbf[:, :].bitcast(F32)
    identb = sbA("identb", [128, 128], BF16)
    q2r = sbA("q2r", [128, T], BF16)
    zr32 = sbA("zr32", [32, 1024], BF16)
    ones32 = sbA("ones32", [32, 128], BF16)
    onec64 = sbA("onec64", [64, 1], BF16)
    s1c = sbA("s1c", [128, NHC], F32)
    s2c = sbA("s2c", [128, ND], F32)
    h1c = sbA("h1c", [128, NHC], F32); h1cn = sbA("h1cn", [128, NHC], F32)
    h2c = sbA("h2c", [128, ND], F32); h2cn = sbA("h2cn", [128, ND], F32)
    sxic = sbA("sxic", [128, NT], F32)
    sxc = sbA("sxc", [128, NT], F32)
    mxh = sbA("mxh", [128, NT], F32)
    sx2cc = sbA("sx2cc", [128, NT], F32)
    sx2ic = sbA("sx2ic", [128, NT], F32)
    chi = sbA("chi", [128, NT], BF16); clo = sbA("clo", [128, NT], BF16)
    beta = sbA("beta", [128, 1], F32)
    colA = sbA("colA", [128, 1], F32); colB = sbA("colB", [128, 1], F32)
    cellA = sbA("cellA", [1, 1], F32)
    bqc = sbA("bqc", [128, ND], F32); bkc = sbA("bkc", [128, ND], F32)
    boc = sbA("boc", [128, ND], F32)
    rowB = sbA("rowB", [1, 512], F32)
    Bt1 = sbA("Bt1", [128, T], F32)
    Bt2 = sbA("Bt2", [128, T], F32)
    Bt3 = sbA("Bt3", [128, T], F32)
    ptl = sbA("ptl", [128, 128], F32)
    jnkv = sbA("jnkv", [128, 32], F32)
    jnks = sbA("jnks", [128, 32], F32)
    jnkg = sbA("jnkg", [128, 32], F32)

    def af(i, w=2048):
        return AF1[:, i * 2048:i * 2048 + w]
    E1, E2, E3, RM, X0, X1 = range(6)
    RNAME = ["E1", "E2", "E3", "RM", "X0", "X1"]
    rowA = AF1[0:1, E2 * 2048:E2 * 2048 + 2048]
    rowC = yst[0:1, T:T + 2048]

    psA = nc.alloc_psum_tensor
    PA = psA("PA", [128, 2048], F32)
    PB = psA("PB", [128, 1536], F32)
    PT = psA("PT", [128, 1024], BF16)

    p = Plan(nc, validate=validate)
    for s in ("TE", "VE", "SE", "GE"):
        p.sem(s)

    id_d = nc.inline_tensor(np.eye(128, dtype=np.float32), name="ident")
    p.dma("sync", af(E1, 128), id_d.ap()[:], chan="init1")
    p.emit("vector", lambda e: e.tensor_copy(identb[:], af(E1, 128)),
           wait=[p.qc("init1")], inc="VE")
    p.emit("vector", lambda e: e.memset(colA[:], 0.25), inc="VE")
    p.emit("vector", lambda e: e.tensor_scalar(colB[:], colA[:], C23, C23,
                                               ALU.add, ALU.subtract), inc="VE")
    p.emit("vector", lambda e: e.tensor_tensor(beta[:], colB[:], colA[:],
                                               ALU.not_equal), inc="VE")
    p.emit("vector", lambda e: e.memset(ones32[:], 1.0), inc="VE")
    p.emit("vector", lambda e: e.memset(onec64[:], 1.0), inc="VE")
    p.emit("vector", lambda e: e.memset(zr32[:], 0.0), inc="VE")
    p.emit("vector", lambda e: e.memset(jnks[:], 0.0), inc="VE")
    init_ve = p.c("VE")
    if "bias" not in DIAG_SKIP:
        for bb_d, bb_s in ((bq_d, bqc), (bk_d, bkc), (bo_d, boc)):
            p.dma("sync", bb_s[:, :], bb_d.rearrange("(c p) -> p c", p=128),
                  chan="bias", slow=True)
    p.rel("E1", ("VE", init_ve))

    def spacer():
        p.emit("tensor", lambda e: e.transpose(
            PT[0:1, 896:1024], identb[:, 0:1], identb[:]), inc="TE")
        return p.c("TE")

    def vpad(k=2):
        for _ in range(k):
            p.emit("vector", lambda e: e.memset(jnkv[:], 0.0), inc="VE")
        return p.c("VE")

    _sfirst = [True]

    def spad(k=2):
        for _ in range(k):
            w = [("VE", init_ve)] if _sfirst[0] else ()
            _sfirst[0] = False
            p.emit("scalar", lambda e: e.activation(
                jnks[:], jnks[:], ACT.Identity, bias=0.0, scale=1.0),
                wait=w, inc="SE")
        return p.c("SE")

    def gpad(k=2):
        for _ in range(k):
            p.emit("gpsimd", lambda e: e.memset(jnkg[:], 0.0), inc="GE")
        return p.c("GE")

    def bcast_rows(src_row, n, dst, extra_wait=(), dst_res=None):
        p.emit("vector", lambda e: e.scalar_tensor_tensor(
            rowC[0:1, 0:n], src_row, 0.0, src_row, ALU.mult, ALU.add),
            wait=list(extra_wait) + p.acq("RWC", "YST1") + [("VE", init_ve)],
            inc="VE")
        ve_hic = p.c("VE")
        dw = p.acq(dst_res) if dst_res else []
        done = 0
        first = True
        se_prev = 0
        while done < n:
            ww = min(1024, n - done)
            p.emit("vector", lambda e, done=done, ww=ww: e.tensor_copy(
                zr32[0:1, 0:ww], rowC[0:1, done:done + ww]),
                wait=([("VE", ve_hic)] + p.acq("ZR")) if first else
                [("TE", p.c("TE"))], inc="VE")
            vh = vpad()
            for j in range((ww + 511) // 512):
                w = min(512, ww - j * 512)
                wv = []
                if j == 0:
                    wv = [("VE", vh)] + p.acq("PB0") + \
                        (dw if first else [("SE", se_prev)])
                p.emit("tensor", lambda e, j=j, w=w: e.matmul(
                    PB[:, j * 512:j * 512 + w], ones32[:],
                    zr32[0:32, j * 512:j * 512 + w],
                    start=True, stop=True), wait=wv, inc="TE")
            p.emit("scalar", lambda e, done=done, ww=ww: e.copy(
                dst[:, done:done + ww], PB[:, 0:ww]),
                wait=[("TE", spacer())], inc="SE")
            se_prev = p.c("SE")
            done += ww
            first = False
        done = 0
        ve_prev = 0
        while done < n:
            ww = min(1024, n - done)
            p.emit("vector", lambda e, done=done, ww=ww: e.tensor_tensor(
                zr32[0:1, 0:ww],
                src_row[0:1, done:done + ww] if hasattr(src_row, "shape")
                else src_row,
                rowC[0:1, done:done + ww], ALU.subtract),
                wait=[("SE", se_prev)] if done == 0 else [("TE", p.c("TE"))],
                inc="VE")
            vh = vpad()
            for j in range((ww + 511) // 512):
                w = min(512, ww - j * 512)
                wv = [("VE", vh)] if j == 0 else []
                p.emit("tensor", lambda e, j=j, w=w: e.matmul(
                    PB[:, j * 512:j * 512 + w], ones32[:],
                    zr32[0:32, j * 512:j * 512 + w],
                    start=True, stop=True), wait=wv, inc="TE")
            p.emit("vector", lambda e, done=done, ww=ww: e.tensor_tensor(
                dst[:, done:done + ww], dst[:, done:done + ww], PB[:, 0:ww],
                ALU.add), wait=[("TE", spacer())], inc="VE")
            ve_prev = p.c("VE")
            done += ww
        p.rel("PB0", ("VE", p.c("VE")))
        p.rel("ZR", ("VE", p.c("VE")))
        p.rel("RWC", ("VE", p.c("VE")))
        p.relmax("YST1", ("VE", p.c("VE")))
        return p.c("VE")

    def cols_to_row(cols, ncols, out_row, wait=()):
        p.emit("scalar", lambda e: e.copy(chi[:, 0:ncols], cols[:, 0:ncols]),
               wait=list(wait) + p.acq("CHL"), inc="SE")
        spad()
        p.emit("vector", lambda e: e.scalar_tensor_tensor(
            clo[:, 0:ncols], chi[:, 0:ncols], -1.0, cols[:, 0:ncols],
            ALU.mult, ALU.add), wait=[("SE", p.c("SE"))], inc="VE")
        ve = vpad()
        p.emit("tensor", lambda e: e.transpose(PT[0:ncols, 0:128],
                                               chi[:, 0:ncols], identb[:]),
               wait=[("VE", ve)] + p.acq("PT"), inc="TE")
        p.emit("tensor", lambda e: e.transpose(PT[0:ncols, 128:256],
                                               clo[:, 0:ncols], identb[:]),
               inc="TE")
        te = spacer()
        p.rel("CHL", ("TE", te))
        p.emit("scalar", lambda e: e.copy(ptl[0:ncols, 0:128],
                                          PT[0:ncols, 128:256]),
               wait=[("TE", te)], inc="SE")
        spad()
        p.emit("vector", lambda e: e.tensor_tensor(
            ptl[0:ncols, 0:128], PT[0:ncols, 0:128], ptl[0:ncols, 0:128],
            ALU.add), wait=[("SE", p.c("SE"))], inc="VE")
        ve2 = p.c("VE")
        p.dma("sync", out_row.rearrange("o (a c) -> o a c", a=ncols),
              ptl[0:ncols, 0:128], chan="c2r", wait=[("VE", ve2)])
        pair = p.qc("c2r")
        p.rel("PT", ("VE", ve2), pair)
        return pair

    def build_B_from_cols(cols, ncols, dst, wait=(), dst_res=None):
        qpair = cols_to_row(cols, ncols, rowA,
                            wait=list(wait) + p.acq("RWA", "E2"))
        r = bcast_rows(rowA[0:1, 0:ncols * 128], ncols * 128, dst,
                       extra_wait=[qpair], dst_res=dst_res)
        p.rel("RWA", ("VE", r))
        p.relmax("E2", ("VE", r))
        return r

    def ternarize(w_src, n_tiles, tdst, scol, hcol, hcoln, tag):
        D2 = w_src.shape[1]
        npc = (D2 + 2047) // 2048
        if "lowregs" in DIAG_SKIP:
            regs = [E1, E2]
            cofs = (X0, X1)
            cres = ("X0", "X1")
        else:
            regs = [X0, X1, RM, E3]
            cofs = (E1, E2)
            cres = ("E1", "E2")
        for i in range(n_tiles):
            pieces = []
            for pc in range(npc):
                r = regs[(i * npc + pc) % len(regs)]
                wt = af(r, min(2048, D2 - pc * 2048))
                ch = "tw%d" % ((i * npc + pc) % len(regs))
                p.dma("sync", wt,
                      w_src[i * 128:(i + 1) * 128,
                            pc * 2048:min(D2, (pc + 1) * 2048)],
                      chan=ch, wait=p.acq(RNAME[r]))
                pieces.append((r, wt, pc, p.qc(ch)))
            for idx, (r, wt, pc, sq) in enumerate(pieces):
                tgt = colA if idx == 0 else colB
                tres = "COLA" if idx == 0 else "COLB"
                p.emit("vector", lambda e, wt=wt, tgt=tgt: e.tensor_reduce(
                    tgt[:], wt, AX.X, ALU.add, apply_absolute_value=True),
                    wait=[sq, ("VE", init_ve)] + p.acq(tres), inc="VE")
                if idx > 0:
                    p.emit("vector", lambda e: e.tensor_tensor(
                        colA[:], colA[:], colB[:], ALU.add), inc="VE")
            p.emit("vector", lambda e, i=i, scol=scol: e.tensor_scalar(
                scol[:, i:i + 1], colA[:], 1.0 / D2, 1e-5, ALU.mult, ALU.max),
                inc="VE")
            p.rel("COLA", ("VE", p.c("VE")))
            p.rel("COLB", ("VE", p.c("VE")))
            p.emit("vector", lambda e, i=i, scol=scol, hcol=hcol: e.tensor_scalar(
                hcol[:, i:i + 1], scol[:, i:i + 1], 0.5, None, ALU.mult),
                inc="VE")
            p.emit("vector", lambda e, i=i, hcol=hcol, hcoln=hcoln: e.tensor_scalar(
                hcoln[:, i:i + 1], hcol[:, i:i + 1], -1.0, None, ALU.mult),
                inc="VE")
            ve = p.c("VE")
            for (r, wt, pc, sq) in pieces:
                W2 = wt.shape[1]
                p.emit("vector", lambda e, wt=wt, i=i, hcol=hcol, W2=W2:
                       e.tensor_tensor(af(cofs[0], W2), wt,
                                       hcol[:, i:i + 1].broadcast_to(
                                           [128, W2]), ALU.is_ge),
                       wait=p.acq(cres[0]), inc="VE")
                p.emit("vector", lambda e, wt=wt, i=i, hcoln=hcoln, W2=W2:
                       e.tensor_tensor(af(cofs[1], W2), wt,
                                       hcoln[:, i:i + 1].broadcast_to(
                                           [128, W2]), ALU.is_le),
                       wait=p.acq(cres[1]), inc="VE")
                ve2 = p.c("VE")
                p.rel(RNAME[r], ("VE", ve2))
                p.emit("vector", lambda e, W2=W2: e.tensor_tensor(
                    yst[:, 0:W2], af(cofs[0], W2), af(cofs[1], W2),
                    ALU.subtract),
                    wait=p.acq("YST0"), inc="VE")
                ve3 = p.c("VE")
                p.rel(cres[0], ("VE", ve3))
                p.rel(cres[1], ("VE", ve3))
                p.dma("gpsimd",
                      tdst[i * 128:(i + 1) * 128,
                           pc * 2048:min(D2, (pc + 1) * 2048)],
                      yst[:, 0:W2], chan="st_" + tag, wait=[("VE", ve3)])
                p.rel("YST0", p.qc("st_" + tag))
        return p.qc("st_" + tag)

    tern1_q = ternarize(w1_d, NHC, t1n, s1c, h1c, h1cn, "w1")
    if "tern2" not in DIAG_SKIP:
        tern2_q = ternarize(w2_d, ND, t2n, s2c, h2c, h2cn, "w2")
        p.emit("vector", lambda e: e.tensor_scalar(
            s2c[:, :], s2c[:, :], 0.5, None, ALU.mult),
            wait=[tern2_q], inc="VE")
    else:
        tern2_q = tern1_q

    def make_transposed(src, dst, tag, wait_q):
        A, B = dst.shape
        for i in range(A // 128):
            for j in range(B // 2048):
                p.dma("scalar", q2r[:, 0:2048],
                      src[j * 2048:(j + 1) * 2048, i * 128:(i + 1) * 128],
                      chan="mtl_" + tag, wait=p.acq("Q2R") + [wait_q],
                      transpose=True)
                p.emit("gpsimd", lambda e: e.memset(jnkg[:], 0.0),
                       wait=[p.qc("mtl_" + tag)], inc="GE")
                gpad(3)
                p.dma("gpsimd",
                      dst[i * 128:(i + 1) * 128, j * 2048:(j + 1) * 2048],
                      q2r[:, 0:2048], chan="st_" + tag,
                      wait=[("GE", p.c("GE"))])
                p.rel("Q2R", p.qc("st_" + tag))
        return p.qc("st_" + tag)

    if "mtr" not in DIAG_SKIP:
        t1t_q = make_transposed(t1n, t1t, "t1t", tern1_q)
        t2t_q = make_transposed(t2n, t2t, "t2t", tern2_q)
    else:
        t1t_q = t2t_q = tern1_q

    def front_end(src, quant, tag, src_q=None):
        for i in range(NT):
            par = i % 2
            xr = RNAME[X0 + par]
            xt = af(X0 + par)
            zr = RNAME[E1 + 2 * par]
            ar = ("E2", "RM")[par]
            zb = af(E1 + 2 * par)
            ab = af(E2) if par == 0 else af(RM)
            fch = "fx%d" % par
            p.dma("sync", xt, src[i * 128:(i + 1) * 128, :], chan=fch,
                  wait=p.acq(xr) + (src_q if (src_q and i < 2) else []))
            sq = p.qc(fch)
            p.emit("vector", lambda e, xt=xt: e.tensor_reduce(
                colA[:], xt, AX.X, ALU.add),
                wait=[sq, ("VE", init_ve)] + p.acq("COLA"), inc="VE")
            p.emit("vector", lambda e: e.tensor_scalar(
                colA[:], colA[:], 1.0 / D, None, ALU.mult), inc="VE")
            ve_m = vpad()
            p.emit("gpsimd", lambda e, xt=xt, zb=zb: e.tensor_scalar(
                zb, xt, colA[:], None, ALU.subtract),
                wait=[("VE", ve_m)] + p.acq(zr), inc="GE")
            ge_z = p.c("GE")
            p.rel("COLA", ("GE", ge_z))
            p.emit("vector", lambda e, zb=zb, ab=ab: e.scalar_tensor_tensor(
                ab, zb, 1.0, zb, ALU.mult, ALU.mult, accum_out=colB[:]),
                wait=[("GE", ge_z)] + p.acq(ar, "COLB"), inc="VE")
            p.emit("vector", lambda e: e.tensor_scalar(
                colB[:], colB[:], 1.0 / D, LN_EPS, ALU.mult, ALU.add), inc="VE")
            vpad()
            p.emit("scalar", lambda e: e.activation(colB[:], colB[:], ACT.Sqrt),
                   wait=[("VE", p.c("VE"))], inc="SE")
            spad()
            p.emit("vector", lambda e: e.reciprocal(colB[:], colB[:]),
                   wait=[("SE", p.c("SE"))], inc="VE")
            ve_r = vpad()
            p.emit("gpsimd", lambda e, zb=zb: e.tensor_scalar(
                zb, zb, colB[:], None, ALU.mult),
                wait=[("VE", ve_r)], inc="GE")
            ge_xn = p.c("GE")
            p.rel("COLB", ("GE", ge_xn))
            ystr = "YST0" if par == 0 else "YST1"
            ytile = yst[:, par * T:par * T + D]
            if not quant:
                p.emit("vector", lambda e, ytile=ytile, zb=zb: e.tensor_copy(
                    ytile, zb), wait=[("GE", ge_xn)] + p.acq(ystr), inc="VE")
                ve_y = p.c("VE")
                p.rel(zr, ("VE", ve_y))
                p.rel(xr, ("GE", ge_z))
                p.rel(ar, ("VE", p.c("VE")))
                p.dma("gpsimd", ascr[i * 128:(i + 1) * 128, :], ytile,
                      chan="st_ascr%d" % par, wait=[("VE", ve_y)])
                p.rel(ystr, p.qc("st_ascr%d" % par))
                continue
            p.emit("vector", lambda e, zb=zb: e.tensor_reduce(
                colA[:], zb, AX.X, ALU.max, apply_absolute_value=True),
                wait=[("GE", ge_xn)] + p.acq("COLA"), inc="VE")
            p.emit("vector", lambda e: e.tensor_scalar(
                colA[:], colA[:], 1e-5, None, ALU.max), inc="VE")
            p.emit("vector", lambda e, i=i: e.tensor_scalar(
                sxic[:, i:i + 1], colA[:], 1.0 / 127.0, None, ALU.mult), inc="VE")
            p.emit("vector", lambda e: e.reciprocal(colB[:], colA[:]),
                   wait=p.acq("COLB"), inc="VE")
            p.emit("vector", lambda e, i=i: e.tensor_scalar(
                sxc[:, i:i + 1], colB[:], 127.0, None, ALU.mult), inc="VE")
            ve_s = vpad()
            p.rel("COLA", ("VE", ve_s))
            p.emit("gpsimd", lambda e, i=i, zb=zb: e.tensor_scalar(
                zb, zb, sxc[:, i:i + 1], None, ALU.mult),
                wait=[("VE", ve_s)], inc="GE")
            ge_v = p.c("GE")
            p.rel("COLB", ("GE", ge_v))
            p.rel(xr, ("GE", ge_z))
            p.emit("vector", lambda e, zb=zb, ab=ab: e.tensor_scalar(
                ab, zb, C23, C23, ALU.add, ALU.subtract),
                wait=[("GE", ge_v)], inc="VE")
            p.emit("vector", lambda e, zb=zb: e.tensor_scalar(
                zb, zb, C52, C52, ALU.add, ALU.subtract), inc="VE")
            p.emit("vector", lambda e, ytile=ytile, zb=zb, ab=ab:
                   e.scalar_tensor_tensor(ytile, ab, beta[:], zb,
                                          ALU.mult, ALU.add),
                   wait=p.acq(ystr), inc="VE")
            ve_y = p.c("VE")
            p.rel(zr, ("VE", ve_y))
            p.rel(ar, ("VE", ve_y))
            p.dma("gpsimd", ascr[i * 128:(i + 1) * 128, :], ytile,
                  chan="st_ascr%d" % par, wait=[("VE", ve_y)])
            p.rel(ystr, p.qc("st_ascr%d" % par))
        ascr_done = [p.qc("st_ascr0"), p.qc("st_ascr1")]
        ab1w = p.acq("AB1")
        for dc in range(ND):
            p.dma("scalar", AB1[:, dc * T:(dc + 1) * T],
                  ascr[:, dc * 128:(dc + 1) * 128], chan="ld_fm",
                  wait=(ascr_done + ab1w) if dc == 0 else (),
                  transpose=True)
        return p.qc("ld_fm")

    def resid_pass(src, dst, y_qs, tag, src_q=None):
        for i in range(NT):
            par = i % 2
            xr = RNAME[X0 + par]
            xt = af(X0 + par)
            ystr = "YST0" if par == 0 else "YST1"
            ytile = yst[:, par * T:par * T + D]
            p.dma("sync", xt, src[i * 128:(i + 1) * 128, :], chan="rx%d" % par,
                  wait=p.acq(xr) + (src_q if (src_q and i < 2) else []))
            sq = p.qc("rx%d" % par)
            p.dma("scalar", ytile, yscr[:, i * 128:(i + 1) * 128],
                  chan="ry%d" % par,
                  wait=p.acq(ystr) + y_qs, transpose=True)
            cq = p.qc("ry%d" % par)
            p.emit("vector", lambda e, xt=xt, ytile=ytile: e.tensor_tensor(
                xt, xt, ytile, ALU.add),
                wait=[sq, cq], inc="VE")
            ve = p.c("VE")
            p.rel(ystr, ("VE", ve))
            p.dma("gpsimd", dst[i * 128:(i + 1) * 128, :], xt,
                  chan="st_x%d" % par, wait=[("VE", ve)])
            p.rel(xr, p.qc("st_x%d" % par))
        return [p.qc("st_x0"), p.qc("st_x1")]

    def ffn_phase(src, dst, tag, src_q=None):
        qa_q = front_end(src, True, tag + "f", src_q=src_q)
        if SUBSTOP == "fe":
            return [qa_q]
        build_B_from_cols(sxic, NT, Bt1, wait=[("VE", p.c("VE"))],
                          dst_res="BT1")
        if SUBSTOP == "bt1":
            return [("VE", p.c("VE"))]
        p.emit("vector", lambda e: e.memset(af(RM), -1e30),
               wait=p.acq("RM"), inc="VE")
        ve_R0 = p.c("VE")

        sess_prev = {}
        n_sg = 1 if SUBSTOP == "m1a" else 16
        for sg in range(n_sg):
            wv = p.acq("AB2") + [t1t_q, qa_q]
            for dc in range(ND):
                p.dma("sync", AB2[:, dc * 512:(dc + 1) * 512],
                      t1t[dc * 128:(dc + 1) * 128, sg * 512:(sg + 1) * 512],
                      chan="m1w", wait=wv if dc == 0 else ())
            p.emit("tensor", lambda e: e.transpose(
                PT[0:1, 896:1024], identb[:, 0:1], identb[:]),
                wait=[p.qc("m1w")], inc="TE")
            p.emit("tensor", lambda e: e.transpose(
                PT[0:1, 896:1024], identb[:, 0:1], identb[:]), inc="TE")
            sq_g = ("TE", p.c("TE"))
            for hh in range(4):
                hc = sg * 4 + hh
                for th in range(2):
                    sess = hc * 2 + th
                    sl = sess % 3
                    pr = ("PA0", "PA1", "PB0")[sl]
                    pp = (PA[:, 0:1024], PA[:, 1024:2048], PB[:, 0:1024])[sl]
                    wv2 = [sq_g] + p.acq(pr)
                    for dc in range(ND):
                        stat = AB2[:, dc * 512 + hh * 128:
                                   dc * 512 + hh * 128 + 128]
                        for tq in range(2):
                            mov = AB1[:, dc * T + th * 1024 + tq * 512:
                                      dc * T + th * 1024 + (tq + 1) * 512]
                            p.emit("tensor", lambda e, pp=pp, stat=stat,
                                   mov=mov, dc=dc, tq=tq: e.matmul(
                                       pp[:, tq * 512:(tq + 1) * 512], stat,
                                       mov, start=(dc == 0),
                                       stop=(dc == ND - 1)),
                                   wait=wv2 if (dc == 0 and tq == 0) else ())
                    te_mm = spacer()
                    hr = "HST0" if sess % 2 == 0 else "HST1"
                    hdst = hst[:, (sess % 2) * 1024:(sess % 2) * 1024 + 1024]
                    p.emit("vector", lambda e, hdst=hdst, pp=pp: e.tensor_copy(
                        hdst, pp), wait=[("TE", te_mm)] + p.acq(hr), inc="VE")
                    ve_c = p.c("VE")
                    p.dma("gpsimd",
                          hsp[hc * 128:(hc + 1) * 128,
                              th * 1024:(th + 1) * 1024],
                          hdst, chan="st_hsp%d" % (sess % 2),
                          wait=[("VE", ve_c)])
                    p.rel(hr, p.qc("st_hsp%d" % (sess % 2)))
                    p.emit("scalar", lambda e, pp=pp, hc=hc: e.activation(
                        af(E1, 1024), pp, ACT.Identity, bias=0.0,
                        scale=s1c[:, hc:hc + 1]),
                        wait=[("VE", ve_c), ("VE", ve_R0)] + p.acq("E1"),
                        inc="SE")
                    se_s = p.c("SE")
                    p.emit("vector", lambda e, th=th: e.tensor_tensor(
                        AF1[:, RM * 2048 + th * 1024:
                            RM * 2048 + (th + 1) * 1024],
                        AF1[:, RM * 2048 + th * 1024:
                            RM * 2048 + (th + 1) * 1024],
                        af(E1, 1024), ALU.max),
                        wait=[("SE", se_s)], inc="VE")
                    ve_mx = p.c("VE")
                    p.rel("E1", ("VE", ve_mx))
                    p.rel(pr, ("VE", max(ve_c, ve_mx)), ("SE", se_s))
            p.rel("AB2", ("TE", p.c("TE")))
        m1_ve = p.c("VE")
        m1_q0 = p.qc("st_hsp0")
        m1_q1 = p.qc("st_hsp1")
        p.rel("AB1", ("TE", p.c("TE")))
        if SUBSTOP in ("m1", "m1a"):
            return [m1_q0, m1_q1, ("VE", m1_ve)]

        p.emit("scalar", lambda e: e.copy(yst[:, 0:T], af(RM)),
               wait=[("VE", m1_ve)] + p.acq("YST0"), inc="SE")
        p.emit("vector", lambda e: e.scalar_tensor_tensor(
            yst[:, T:2 * T], yst[:, 0:T], -1.0, af(RM), ALU.mult, ALU.add),
            wait=[("SE", p.c("SE"))] + p.acq("YST1"), inc="VE")
        for ch in range(NT):
            ve_in = p.c("VE")
            p.emit("tensor", lambda e, ch=ch: e.transpose(
                PT[:, 0:128], yst[:, ch * 128:(ch + 1) * 128], identb[:]),
                wait=[("VE", ve_in)] + p.acq("PT"), inc="TE")
            p.emit("tensor", lambda e, ch=ch: e.transpose(
                PT[:, 128:256], yst[:, T + ch * 128:T + (ch + 1) * 128],
                identb[:]), inc="TE")
            te_t = spacer()
            p.emit("scalar", lambda e: e.copy(ptl[:, 0:128],
                                              PT[:, 128:256]),
                   wait=[("TE", te_t)], inc="SE")
            spad()
            p.emit("vector", lambda e: e.tensor_tensor(
                af(E1, 128), PT[:, 0:128], ptl[:, 0:128], ALU.add),
                wait=[("SE", p.c("SE"))] + p.acq("E1"), inc="VE")
            p.emit("vector", lambda e, ch=ch: e.tensor_reduce(
                mxh[:, ch:ch + 1], af(E1, 128), AX.X, ALU.max), inc="VE")
            p.rel("E1", ("VE", p.c("VE")))
            p.rel("PT", ("VE", p.c("VE")))
        p.rel("YST0", ("TE", p.c("TE")))
        p.rel("YST1", ("TE", p.c("TE")))
        p.rel("RM", ("VE", p.c("VE")))
        p.emit("vector", lambda e: e.tensor_tensor(
            af(E1, NT), mxh[:, :], sxic[:, :], ALU.mult),
            wait=p.acq("E1"), inc="VE")
        vpad()
        p.emit("scalar", lambda e: e.activation(
            af(E2, NT), af(E1, NT), ACT.Gelu_apprx_tanh),
            wait=[("VE", p.c("VE"))] + p.acq("E2"), inc="SE")
        spad()
        p.emit("vector", lambda e: e.tensor_scalar(
            af(E2, NT), af(E2, NT), 1e-5, None, ALU.max),
            wait=[("SE", p.c("SE"))], inc="VE")
        p.emit("vector", lambda e: e.reciprocal(af(E3, NT), af(E2, NT)),
               wait=p.acq("E3"), inc="VE")
        p.emit("vector", lambda e: e.tensor_scalar(
            sx2cc[:, :], af(E3, NT), 127.0, None, ALU.mult), inc="VE")
        p.emit("vector", lambda e: e.reciprocal(sx2ic[:, :], sx2cc[:, :]),
               inc="VE")
        sx2_ve = vpad()
        p.rel("E1", ("SE", p.c("SE")))
        p.rel("E2", ("VE", sx2_ve))
        p.rel("E3", ("VE", sx2_ve))
        build_B_from_cols(sx2cc, NT, Bt2, wait=[("VE", sx2_ve)], dst_res="BT2")
        build_B_from_cols(sx2ic, NT, Bt3, dst_res="BT3")
        bs_ve = p.c("VE")
        if SUBSTOP == "sx2":
            return [("VE", bs_ve)]

        for hhf in range(2):
            for thf in range(2):
                tt0 = thf * 1024
                ab1w = p.acq("AB1")
                for hh in range(32):
                    hc = hhf * 32 + hh
                    par = hh % 2
                    hr = "HST0" if par == 0 else "HST1"
                    hl = hst[:, par * 1024:par * 1024 + 1024]
                    e1r, e2r, e3r = (("E1", "E2", "E3"), ("RM", "X0", "X1"))[par]
                    e1 = af((E1, RM)[par], 1024)
                    e2 = af((E2, X0)[par], 1024)
                    e3 = af((E3, X1)[par], 1024)
                    p.dma("sync", hl,
                          hsp[hc * 128:(hc + 1) * 128, tt0:tt0 + 1024],
                          chan="p4h%d" % par, wait=p.acq(hr) + [m1_q0, m1_q1])
                    sq_h = p.qc("p4h%d" % par)
                    p.emit("gpsimd", lambda e, hl=hl, e1=e1: e.tensor_copy(e1, hl),
                           wait=[sq_h] + p.acq(e1r), inc="GE")
                    ge_c = p.c("GE")
                    p.rel(hr, ("GE", ge_c))
                    p.emit("scalar", lambda e, hc=hc, e1=e1: e.activation(
                        e1, e1, ACT.Identity, bias=0.0,
                        scale=s1c[:, hc:hc + 1]),
                        wait=[("GE", ge_c), ("VE", bs_ve)], inc="SE")
                    p.emit("vector", lambda e, e1=e1, tt0=tt0: e.tensor_tensor(
                        e1, e1, Bt1[:, tt0:tt0 + 1024], ALU.mult),
                        wait=[("SE", p.c("SE"))], inc="VE")
                    p.emit("scalar", lambda e, e1=e1, e2=e2: e.activation(
                        e2, e1, ACT.Gelu_apprx_tanh),
                        wait=[("VE", p.c("VE"))] + p.acq(e2r), inc="SE")
                    p.emit("vector", lambda e, e2=e2, tt0=tt0: e.tensor_tensor(
                        e2, e2, Bt2[:, tt0:tt0 + 1024], ALU.mult),
                        wait=[("SE", p.c("SE"))], inc="VE")
                    p.rel(e1r, ("VE", p.c("VE")))
                    p.emit("vector", lambda e, e2=e2, e3=e3: e.tensor_scalar(
                        e3, e2, C23, C23, ALU.add, ALU.subtract),
                        wait=p.acq(e3r), inc="VE")
                    p.emit("vector", lambda e, e2=e2: e.tensor_scalar(
                        e2, e2, C52, C52, ALU.add, ALU.subtract), inc="VE")
                    p.emit("vector", lambda e, hh=hh, e2=e2, e3=e3:
                           e.scalar_tensor_tensor(
                               AB1[:, hh * 1024:(hh + 1) * 1024], e3, beta[:],
                               e2, ALU.mult, ALU.add),
                           wait=ab1w if hh == 0 else (), inc="VE")
                    ve_gq = p.c("VE")
                    p.rel(e2r, ("VE", ve_gq))
                    p.rel(e3r, ("VE", ve_gq))
                p4_ve = p.c("VE")
                for t5 in range(2):
                    for cq in range(4):
                        for hb in range(8):
                            wsl = hb % 2
                            wr = "AB3A" if wsl == 0 else "AB3B"
                            wdst = AB3[:, wsl * 2048:(wsl + 1) * 2048]
                            hc0 = hhf * 32 + hb * 4
                            src_ap = t2t[hc0 * 128:(hc0 + 4) * 128,
                                         cq * 512:(cq + 1) * 512]
                            src_ap = src_ap.rearrange("(k p) c -> p k c",
                                                      p=128)
                            p.dma("sync", wdst.rearrange("p (k c) -> p k c", k=4),
                                  src_ap, chan="m2w%d" % wsl,
                                  wait=p.acq(wr) + [t2t_q, ("VE", p4_ve)])
                            p.emit("tensor", lambda e: e.transpose(
                                PT[0:1, 896:1024], identb[:, 0:1], identb[:]),
                                wait=[p.qc("m2w%d" % wsl)], inc="TE")
                            p.emit("tensor", lambda e: e.transpose(
                                PT[0:1, 896:1024], identb[:, 0:1],
                                identb[:]), inc="TE")
                            sq_w = ("TE", p.c("TE"))
                            wfirst = p.acq("PA0", "PA1") if hb == 0 else []
                            for k in range(4):
                                hh2 = hb * 4 + k
                                for cc in range(4):
                                    stat = AB3[:, wsl * 2048 + k * 512 +
                                               cc * 128:wsl * 2048 + k * 512 +
                                               (cc + 1) * 128]
                                    mov = AB1[:, hh2 * 1024 + t5 * 512:
                                              hh2 * 1024 + (t5 + 1) * 512]
                                    p.emit("tensor", lambda e, stat=stat,
                                           mov=mov, cc=cc, hh2=hh2: e.matmul(
                                               PA[:, cc * 512:(cc + 1) * 512],
                                               stat, mov, start=(hh2 == 0),
                                               stop=(hh2 == 31)),
                                           wait=([sq_w] + wfirst)
                                           if (k == 0 and cc == 0) else ())
                                    p.rel(wr, ("TE", p.c("TE")))
                        te_m2 = spacer()
                        evs = []
                        for cc in range(4):
                            c_abs = cq * 4 + cc
                            trow = tt0 + t5 * 512
                            if hhf == 0:
                                p.emit("vector", lambda e, cc=cc: e.tensor_copy(
                                    af(E1, 512), PA[:, cc * 512:(cc + 1) * 512]),
                                    wait=[("TE", te_m2)] + p.acq("E1"), inc="VE")
                                ve_e = p.c("VE")
                                evs.append(("VE", ve_e))
                                p.dma("gpsimd",
                                      pscr[c_abs * 128:(c_abs + 1) * 128,
                                           trow:trow + 512],
                                      af(E1, 512), chan="st_pscr",
                                      wait=[("VE", ve_e)])
                                p.rel("E1", p.qc("st_pscr"))
                            else:
                                psl = cc % 2
                                pr2 = ("X1", "E3")[psl]
                                pld = af((X1, E3)[psl], 512)
                                p.dma("sync", pld,
                                      pscr[c_abs * 128:(c_abs + 1) * 128,
                                           trow:trow + 512],
                                      chan="pc%d" % psl,
                                      wait=p.acq(pr2) + [p.qc("st_pscr")])
                                sq_p = p.qc("pc%d" % psl)
                                p.emit("vector", lambda e, cc=cc, pld=pld:
                                       e.tensor_tensor(
                                           af(E1, 512),
                                           PA[:, cc * 512:(cc + 1) * 512],
                                           pld, ALU.add),
                                       wait=[("TE", te_m2), sq_p] +
                                       p.acq("E1"), inc="VE")
                                ve_a = p.c("VE")
                                evs.append(("VE", ve_a))
                                p.rel(pr2, ("VE", ve_a))
                                p.emit("scalar", lambda e, c_abs=c_abs:
                                       e.activation(
                                           af(E2, 512), af(E1, 512),
                                           ACT.Identity, bias=0.0,
                                           scale=s2c[:, c_abs:c_abs + 1]),
                                       wait=[("VE", ve_a)] + p.acq("E2"),
                                       inc="SE")
                                p.rel("E1", ("SE", p.c("SE")))
                                p.emit("vector", lambda e, trow=trow:
                                       e.tensor_tensor(
                                           yst[:, 0:512], af(E2, 512),
                                           Bt3[:, trow:trow + 512], ALU.mult),
                                       wait=[("SE", p.c("SE"))] +
                                       p.acq("YST0"), inc="VE")
                                ve_y = p.c("VE")
                                p.rel("E2", ("VE", ve_y))
                                p.dma("gpsimd",
                                      yscr[c_abs * 128:(c_abs + 1) * 128,
                                           trow:trow + 512],
                                      yst[:, 0:512], chan="st_yscr0",
                                      wait=[("VE", ve_y)])
                                p.rel("YST0", p.qc("st_yscr0"))
                        p.rel("PA0", *evs)
                        p.rel("PA1", *evs)
                p.rel("AB1", ("TE", p.c("TE")))
        if SUBSTOP == "p4m2":
            return [p.qc("st_yscr0"), p.qc("st_yscr1")]
        return resid_pass(src, dst,
                          [p.qc("st_yscr0"), p.qc("st_yscr1")], tag + "r",
                          src_q=src_q)

    def attn_phase(src, dst, tag, src_q=None):
        xn_q = front_end(src, False, tag + "f", src_q=src_q)
        p.dma("sync", rowA[0:1, :], bv_d.rearrange("(o d) -> o d", o=1),
              chan="bias", wait=p.acq("RWA", "E2"), slow=True)
        bcast_rows(rowA[0:1, :], D, Bt3, extra_wait=[p.qc("bias")] +
                   p.acq("BT3"))
        p.rel("RWA", ("VE", p.c("VE")))
        p.relmax("E2", ("VE", p.c("VE")))

        VQOFF = 8192
        for pair in range(16):
            if pair % 2 == 0:
                quad = pair // 2
                wv = p.acq("X0", "X1")
                p.dma("sync",
                      AF1[:, X0 * 2048:X0 * 2048 + 4096].rearrange(
                          "p (a c) -> p a c", a=ND),
                      wv_d[:, quad * 256:(quad + 1) * 256].rearrange(
                          "(a p) c -> p a c", p=128), chan="awv", wait=wv)
                sq_v = p.qc("awv")
                p.emit("vector", lambda e: e.tensor_copy(
                    wbf[:, 0:4096], AF1[:, X0 * 2048:X0 * 2048 + 4096]),
                    wait=[sq_v] + p.acq("WBF"), inc="VE")
                ve_vw = p.c("VE")
                p.rel("X0", ("VE", ve_vw))
                p.rel("X1", ("VE", ve_vw))
                p.emit("vector", lambda e: e.memset(
                    AB2[:, VQOFF:VQOFF + 4160], 1.0),
                    wait=p.acq("VQT"), inc="VE")
                ve_vm = p.c("VE")
                for tt in range(NT):
                    ppv = PB[:, 0:256]
                    wv2 = [("VE", ve_vw), ("VE", ve_vm), xn_q] + p.acq("PB0")
                    for dc in range(ND):
                        stat = AB1[:, dc * T + tt * 128:dc * T + (tt + 1) * 128]
                        mov = wbf[:, dc * 256:(dc + 1) * 256]
                        p.emit("tensor", lambda e, ppv=ppv, stat=stat, mov=mov,
                               dc=dc: e.matmul(ppv, stat, mov,
                                               start=(dc == 0),
                                               stop=(dc == ND - 1)),
                               wait=wv2 if dc == 0 else ())
                    te_v = spacer()
                    for s4 in range(4):
                        p.emit("vector", lambda e, tt=tt, s4=s4, ppv=ppv,
                               quad=quad: e.tensor_tensor(
                                   AB2[:, VQOFF + tt * 260 + s4 * 65:
                                       VQOFF + tt * 260 + s4 * 65 + 64],
                                   ppv[:, s4 * 64:(s4 + 1) * 64],
                                   Bt3[:, quad * 256 + s4 * 64:
                                       quad * 256 + s4 * 64 + 64],
                                   ALU.add),
                               wait=[("TE", te_v)] if s4 == 0 else (),
                               inc="VE")
                    p.rel("PB0", ("VE", p.c("VE")))
                vq_ve = p.c("VE")
                p.rel("WBF", ("TE", p.c("TE")))

            qcols = slice(pair * 128, (pair + 1) * 128)
            for wtile, wsrc in ((0, wq_d), (1, wk_d)):
                wl = af((X0, X1)[wtile])
                ch = ("awq", "awk")[wtile]
                p.dma("sync", wl.rearrange("p (a c) -> p a c", a=ND),
                      wsrc[:, qcols].rearrange("(a p) c -> p a c", p=128),
                      chan=ch, wait=p.acq(("X0", "X1")[wtile]))
                sq_w = p.qc(ch)
                p.emit("vector", lambda e, wl=wl, wtile=wtile: e.tensor_copy(
                    wbf[:, wtile * D:(wtile + 1) * D], wl),
                    wait=[sq_w] + (p.acq("WBF") if wtile == 0 else []),
                    inc="VE")
                p.rel(("X0", "X1")[wtile], ("VE", p.c("VE")))
            ve_w = p.c("VE")

            for qk in range(2):
                base = qk * 2 * T
                wv = [("VE", ve_w), xn_q, p.qc("bias")] + \
                    p.acq("PA0", "PA1") + \
                    (p.acq("AB3A", "AB3B", "AB3C", "AB3D") if qk == 0 else [])
                for dc in range(ND):
                    stat = wbf[:, qk * D + dc * 128:qk * D + (dc + 1) * 128]
                    for tq in range(4):
                        mov = AB1[:, dc * T + tq * 512:dc * T + (tq + 1) * 512]
                        p.emit("tensor", lambda e, stat=stat, mov=mov, dc=dc,
                               tq=tq: e.matmul(
                                   PA[:, tq * 512:(tq + 1) * 512], stat, mov,
                                   start=(dc == 0), stop=(dc == ND - 1)),
                               wait=wv if (dc == 0 and tq == 0) else ())
                te_p = spacer()
                bias_c = (bqc, bkc)[qk]
                p.emit("scalar", lambda e, base=base, pair=pair, bias_c=bias_c:
                       e.activation(AB3[0:64, base:base + T], PA[0:64, :],
                                    ACT.Identity,
                                    bias=bias_c[0:64, pair:pair + 1],
                                    scale=1.0),
                       wait=[("TE", te_p)], inc="SE")
                p.emit("scalar", lambda e, base=base, pair=pair, bias_c=bias_c:
                       e.activation(AB3[64:128, base + T:base + 2 * T],
                                    PA[64:128, :], ACT.Identity,
                                    bias=bias_c[64:128, pair:pair + 1],
                                    scale=1.0), inc="SE")
                p.rel("PA0", ("SE", p.c("SE")))
                p.rel("PA1", ("SE", p.c("SE")))
            se_qk = p.c("SE")
            for qk in range(2):
                base = qk * 2 * T
                p.dma("scalar", AB3[0:64, base + T:base + 2 * T],
                      AB3[64:128, base + T:base + 2 * T], chan="qsh",
                      wait=[("SE", se_qk)])
            cq_sh = p.qc("qsh")
            p.emit("vector", lambda e: e.memset(AB3[64:65, 2 * T:3 * T], 1.0),
                   wait=[cq_sh], inc="VE")
            p.emit("vector", lambda e: e.memset(AB3[64:65, 3 * T:4 * T], 1.0),
                   inc="VE")
            ve_on = p.c("VE")

            for h in range(2):
                qb = h * T
                kb = 2 * T + h * T
                p.emit("scalar", lambda e, qb=qb: e.activation(
                    q2r[0:64, 0:T], AB3[0:64, qb:qb + T], ACT.Square),
                    wait=[cq_sh, ("VE", ve_on)] + p.acq("Q2R"),
                    inc="SE")
                se_q2 = spad()
                for j in range(4):
                    p.emit("tensor", lambda e, j=j: e.matmul(
                        PA[0:1, j * 512:(j + 1) * 512], onec64[:],
                        q2r[0:64, j * 512:(j + 1) * 512], start=True,
                        stop=True),
                        wait=([("SE", se_q2)] + p.acq("PA0", "PA1"))
                        if j == 0 else (), inc="TE")
                p.emit("vector", lambda e: e.tensor_copy(rowA[0:1, :],
                                                         PA[0:1, 0:T]),
                       wait=[("TE", spacer())] + p.acq("RWA", "E2"), inc="VE")
                ve_qn = p.c("VE")
                p.emit("scalar", lambda e, kb=kb: e.activation(
                    q2r[0:64, 0:T], AB3[0:64, kb:kb + T], ACT.Square),
                    wait=[("VE", ve_qn)], inc="SE")
                se_k2 = spad()
                for j in range(4):
                    p.emit("tensor", lambda e, j=j: e.matmul(
                        PA[0:1, j * 512:(j + 1) * 512], onec64[:],
                        q2r[0:64, j * 512:(j + 1) * 512], start=True,
                        stop=True),
                        wait=[("SE", se_k2), ("VE", ve_qn)] if j == 0 else (),
                        inc="TE")
                p.emit("vector", lambda e: e.tensor_reduce(
                    cellA[0:1, :], PA[0:1, 0:T], AX.X, ALU.max),
                    wait=[("TE", spacer())], inc="VE")
                p.rel("PA0", ("VE", p.c("VE")))
                p.rel("PA1", ("VE", p.c("VE")))
                p.rel("Q2R", ("TE", p.c("TE")))
                p.emit("scalar", lambda e: e.activation(
                    cellA[0:1, :], cellA[0:1, :], ACT.Sqrt),
                    wait=[("VE", p.c("VE"))], inc="SE")
                p.emit("scalar", lambda e: e.activation(
                    rowA[0:1, :], rowA[0:1, :], ACT.Sqrt), inc="SE")
                se_rt = spad()
                p.emit("vector", lambda e: e.tensor_scalar(
                    rowA[0:1, :], rowA[0:1, :], cellA[0:1, 0:1], -1.0,
                    ALU.mult, ALU.mult), wait=[("SE", se_rt)], inc="VE")
                p.emit("scalar", lambda e, qb=qb: e.activation(
                    AB3[64:65, qb:qb + T], rowA[0:1, :], ACT.Identity),
                    wait=[("VE", p.c("VE"))], inc="SE")
                p.rel("RWA", ("SE", p.c("SE")))
                p.relmax("E2", ("SE", p.c("SE")))
            se_m = p.c("SE")

            for h in range(2):
                head = pair * 2 + h
                hs4 = (pair % 2) * 2 + h
                qb = h * T
                kb = 2 * T + h * T
                for tqb in range(4):
                    groups = ((0, 4), (1, 3), (0, 4), (1, 3), (0, 2))
                    chunk = 0
                    for gi, (which, ng) in enumerate(groups):
                        pt = PA if which == 0 else PB
                        pres = ("PA0", "PA1") if which == 0 else ("PB0",)
                        wv = [("SE", se_m)] + p.acq(*pres) + \
                            (p.acq("PRB") if gi == 0 and tqb == 0 else [])
                        for g2 in range(ng):
                            tk = chunk + g2
                            p.emit("tensor", lambda e, pt=pt, g2=g2, tk=tk,
                                   qb=qb, kb=kb, tqb=tqb: e.matmul(
                                       pt[:, g2 * 512:(g2 + 1) * 512],
                                       AB3[0:65, kb + tk * 128:
                                           kb + (tk + 1) * 128],
                                       AB3[0:65, qb + tqb * 512:
                                           qb + (tqb + 1) * 512],
                                       start=True, stop=True),
                                   wait=wv if g2 == 0 else ())
                        te_s = spacer()
                        p.emit("scalar", lambda e, pt=pt, ng=ng, chunk=chunk:
                               e.activation(
                                   AB2[:, chunk * 512:(chunk + ng) * 512],
                                   pt[:, 0:ng * 512], ACT.Exp, bias=0.0,
                                   scale=0.125),
                               wait=[("TE", te_s)], inc="SE")
                        se_e = p.c("SE")
                        for rr in pres:
                            p.rel(rr, ("SE", se_e))
                        chunk += ng
                    se_exp = p.c("SE")
                    wv = [("SE", se_exp), ("VE", vq_ve)] + p.acq("PB6")
                    for tk in range(NT):
                        stat = AB2[:, VQOFF + tk * 260 + hs4 * 65:
                                   VQOFF + tk * 260 + hs4 * 65 + 65]
                        mov = AB2[:, tk * 512:(tk + 1) * 512]
                        p.emit("tensor", lambda e, stat=stat, mov=mov, tk=tk:
                               e.matmul(PB[0:65, 1024:1536], stat, mov,
                                        start=(tk == 0), stop=(tk == NT - 1)),
                               wait=wv if tk == 0 else ())
                    te_c = spacer()
                    p.emit("vector", lambda e: e.reciprocal(
                        rowB[0:1, 0:512], PB[64:65, 1024:1536]),
                        wait=[("TE", te_c)] + p.acq("RWB"), inc="VE")
                    p.emit("vector", lambda e: e.tensor_copy(
                        zr32[0:1, 0:512], rowB[0:1, 0:512]),
                        wait=p.acq("ZR"), inc="VE")
                    ve_l = vpad()
                    p.rel("RWB", ("VE", ve_l))
                    p.emit("tensor", lambda e: e.matmul(
                        PA[0:64, 0:512], ones32[:, 0:64], zr32[0:32, 0:512],
                        start=True, stop=True),
                        wait=[("VE", ve_l)] + p.acq("PA0"), inc="TE")
                    te_b = p.c("TE")
                    p.rel("ZR", ("TE", te_b))
                    p.emit("scalar", lambda e: e.copy(
                        AF1[0:64, E1 * 2048:E1 * 2048 + 512],
                        PB[0:64, 1024:1536]),
                        wait=[("TE", te_b)] + p.acq("E1"), inc="SE")
                    se_cx = p.c("SE")
                    p.rel("PB6", ("SE", se_cx))
                    p.emit("vector", lambda e, tqb=tqb: e.tensor_tensor(
                        q2r[0:64, tqb * 512:(tqb + 1) * 512],
                        AF1[0:64, E1 * 2048:E1 * 2048 + 512],
                        PA[0:64, 0:512], ALU.mult),
                        wait=[("SE", se_cx)] + (p.acq("Q2R") if tqb == 0
                                                else []), inc="VE")
                    ve_cn = p.c("VE")
                    p.rel("E1", ("VE", ve_cn))
                    p.rel("PA0", ("VE", ve_cn))
                p.dma("gpsimd", cscr[head * 64:(head + 1) * 64, :],
                      q2r[0:64, 0:T], chan="st_cscr", wait=[("VE", p.c("VE"))])
                p.rel("Q2R", p.qc("st_cscr"))
            p.rel("PRB", ("TE", p.c("TE")))
            if pair % 2 == 1:
                p.rel("VQT", ("TE", p.c("TE")))
        ctx_q = p.qc("st_cscr")
        ab1_te = p.c("TE")

        for dc in range(ND):
            wl = af((X0, X1)[dc % 2])
            p.dma("sync", wl, wo_d[dc * 128:(dc + 1) * 128, :],
                  chan="awo%d" % (dc % 2),
                  wait=p.acq(("X0", "X1")[dc % 2]) +
                  ([("TE", ab1_te)] if dc < 2 else []))
            sq_o = p.qc("awo%d" % (dc % 2))
            p.emit("vector", lambda e, wl=wl, dc=dc: e.tensor_copy(
                AB1[:, dc * D:(dc + 1) * D], wl),
                wait=[sq_o, ("TE", ab1_te)] +
                (p.acq("AB1") if dc == 0 else []), inc="VE")
            p.rel(("X0", "X1")[dc % 2], ("VE", p.c("VE")))
        wo_ve = p.c("VE")
        for t5 in range(4):
            wv = [ctx_q, ("TE", ab1_te)] + \
                p.acq("AB3A", "AB3B", "AB3C", "AB3D")
            p.dma("sync", AB3[:, 0:8192].rearrange("p (a c) -> p a c", a=ND),
                  cscr[:, t5 * 512:(t5 + 1) * 512].rearrange(
                      "(a p) c -> p a c", p=128), chan="oc", wait=wv)
            sq_c = p.qc("oc")
            for dq in range(4):
                wv2 = [sq_c, ("VE", wo_ve)] + p.acq("PA0", "PA1")
                for dc in range(ND):
                    for c4 in range(4):
                        dout = dq * 4 + c4
                        stat = AB1[:, dc * D + dout * 128:
                                   dc * D + (dout + 1) * 128]
                        mov = AB3[:, dc * 512:(dc + 1) * 512]
                        p.emit("tensor", lambda e, stat=stat, mov=mov, dc=dc,
                               c4=c4: e.matmul(
                                   PA[:, c4 * 512:(c4 + 1) * 512], stat, mov,
                                   start=(dc == 0), stop=(dc == ND - 1)),
                               wait=wv2 if (dc == 0 and c4 == 0) else ())
                te_o = spacer()
                ses = []
                for c4 in range(4):
                    dout = dq * 4 + c4
                    ystr2 = "YST0" if c4 % 2 == 0 else "YST1"
                    p.emit("scalar", lambda e, c4=c4, dout=dout: e.activation(
                        yst[:, (c4 % 2) * T:(c4 % 2) * T + 512],
                        PA[:, c4 * 512:(c4 + 1) * 512], ACT.Identity,
                        bias=boc[:, dout:dout + 1], scale=1.0),
                        wait=[("TE", te_o)] + p.acq(ystr2), inc="SE")
                    se_y = p.c("SE")
                    ses.append(("SE", se_y))
                    p.dma("gpsimd", yscr[dout * 128:(dout + 1) * 128,
                                         t5 * 512:(t5 + 1) * 512],
                          yst[:, (c4 % 2) * T:(c4 % 2) * T + 512],
                          chan="st_yscr%d" % (c4 % 2), wait=[("SE", se_y)])
                    p.rel(ystr2, p.qc("st_yscr%d" % (c4 % 2)))
                p.rel("PA0", *ses)
                p.rel("PA1", *ses)
            for rr in ("AB3A", "AB3B", "AB3C", "AB3D"):
                p.rel(rr, ("TE", p.c("TE")))
        p.rel("AB1", ("TE", p.c("TE")))
        return resid_pass(src, dst,
                          [p.qc("st_yscr0"), p.qc("st_yscr1")], tag + "r",
                          src_q=src_q)

    if SUBSTOP == "dump_t1":
        p.dma("gpsimd", out_d, t1n[0:T, :], chan="dump", wait=[tern1_q])
        fin = [p.qc("dump")]
        p.ops["gpsimd"].append(
            lambda e: [e.wait_ge(p.sems[s], v) for s, v in fin])
        return _finish(nc, p)
    if SUBSTOP == "dump_s1":
        p.dma("gpsimd", out_d[0:128, 0:NHC], s1c[:, :], chan="dump",
              wait=[tern1_q])
        fin = [p.qc("dump")]
        p.ops["gpsimd"].append(
            lambda e: [e.wait_ge(p.sems[s], v) for s, v in fin])
        return _finish(nc, p)
    if SUBSTOP == "dump_t2":
        p.dma("gpsimd", out_d, t2n[:, 0:T], chan="dump", wait=[tern2_q])
        fin = [p.qc("dump")]
        p.ops["gpsimd"].append(
            lambda e: [e.wait_ge(p.sems[s], v) for s, v in fin])
        return _finish(nc, p)

    if phases >= 1:
        a_q = ffn_phase(x_in, xb0 if phases > 1 else out_d, "A")
        fin = list(a_q)
    if phases >= 2:
        b_q = attn_phase(xb0, xb1 if phases > 2 else out_d, "B", src_q=a_q)
        fin = list(b_q)
    if phases >= 3:
        c_q = ffn_phase(xb1, out_d, "C", src_q=b_q)
        fin = list(c_q)
    if phases == 0:
        fin = [t1t_q, t2t_q]
    p.ops["gpsimd"].append(
        lambda e: [e.wait_ge(p.sems[s], v) for s, v in fin])

    return _finish(nc, p)


def _finish(nc, p):
    with nc.Block() as block:
        @block.sync
        def _(e):
            for f in p.ops["sync"]:
                f(e)

        @block.tensor
        def _(e):
            for f in p.ops["tensor"]:
                f(e)

        @block.vector
        def _(e):
            for f in p.ops["vector"]:
                f(e)

        @block.scalar
        def _(e):
            for f in p.ops["scalar"]:
                f(e)

        @block.gpsimd
        def _(e):
            for f in p.ops["gpsimd"]:
                f(e)

    for g in reversed(p._guards):
        g.__exit__(None, None, None)
    return nc


_CACHE = {}


def kernel(**inputs):
    x = np.ascontiguousarray(np.asarray(inputs["L0_attn_hidden_states"],
                                        dtype=np.float32))
    full = {
        "wq": np.ascontiguousarray(np.asarray(inputs["L0_attn_Wq"], np.float32)),
        "wk": np.ascontiguousarray(np.asarray(inputs["L0_attn_Wk"], np.float32)),
        "wv": np.ascontiguousarray(np.asarray(inputs["L0_attn_Wv"], np.float32)),
        "wo": np.ascontiguousarray(np.asarray(inputs["L0_attn_Wo"], np.float32)),
        "bq": np.ascontiguousarray(np.asarray(inputs["L0_attn_bq"], np.float32)),
        "bk": np.ascontiguousarray(np.asarray(inputs["L0_attn_bk"], np.float32)),
        "bv": np.ascontiguousarray(np.asarray(inputs["L0_attn_bv"], np.float32)),
        "bo": np.ascontiguousarray(np.asarray(inputs["L0_attn_bo"], np.float32)),
        "w1": np.ascontiguousarray(np.asarray(inputs["L0_ffn_w1"], np.float32)),
        "w2": np.ascontiguousarray(np.asarray(inputs["L0_ffn_w2"], np.float32)),
    }
    if "nc" not in _CACHE:
        _CACHE["nc"] = build_program()
    nc = _CACHE["nc"]
    in_maps = []
    for c in range(8):
        m = {"x": np.ascontiguousarray(x[c])}
        m.update(full)
        in_maps.append(m)
    res = run_bass_kernel_spmd(nc, in_maps, core_ids=list(range(8)))
    out = np.stack([res.results[c]["out"] for c in range(8)], axis=0)
    return out.astype(np.float32)
